# version 02 (final)
# baseline (speedup 1.0000x reference)
import contextlib
import numpy as np
import ml_dtypes
import concourse.bass as bass
import concourse.mybir as mybir
from concourse.bass_utils import run_bass_kernel_spmd

F32 = mybir.dt.float32
BF16 = mybir.dt.bfloat16
I8 = mybir.dt.int8
I32 = mybir.dt.int32
ALU = mybir.AluOpType
AF = mybir.ActivationFunctionType
AX = mybir.AxisListType
BF = ml_dtypes.bfloat16

NCORE = 8
T = 1024
TT = T // 128
D = 2048
DC = D // 128
MID = 8192
MC = MID // 128
AGW = 4224
EPS_LN = 1e-5
EPS_ATT = 1e-9
EPS_AQ = 1e-5
C_ATT = float(128 ** -0.5)
C_FWHT = float(np.float32(1.0) / np.float32(np.sqrt(np.float32(8192.0))))

DEBUG = False
KCUT = 99


class _VG:
    _ROLES = {
        "tensor_copy": (0, (1,)), "tensor_scalar": (0, (1, 2, 3)),
        "tensor_tensor": (0, (1, 2)), "scalar_tensor_tensor": (0, (1, 2, 3)),
        "tensor_reduce": (0, (1,)), "reduce_max": (0, (1,)), "reduce_sum": (0, (1,)),
        "reciprocal": (0, (1,)), "tensor_tensor_scan": (0, (1, 2, 3)),
        "memset": (0, ()),
    }
    _SLOW = {"tensor_reduce", "reduce_max", "reduce_sum", "reciprocal",
             "tensor_tensor_scan"}
    W_SLOW = 7
    W_FAST = 3
    CREDIT = 2

    def __init__(self, v, st):
        self._v = v
        self._st = st

    @staticmethod
    def _tname(x):
        t = getattr(x, "tensor", None)
        return getattr(t, "name", None) if t is not None else None

    def _decay(self, n):
        hot = self._st["hot"]
        for k in list(hot):
            hot[k] -= n
            if hot[k] <= 0:
                del hot[k]

    def __getattr__(self, name):
        v = self._v
        st = self._st
        fn = getattr(v, name)
        roles = self._ROLES.get(name)
        if roles is None:
            if name == "engine_nop":
                def wrap0(*a, **k):
                    self._decay(1)
                    return fn(*a, **k)
                return wrap0
            return fn
        oidx, iidx = roles

        def wrap(*a, **k):
            hot = st["hot"]
            ins = set()
            for i in iidx:
                if i < len(a):
                    n = self._tname(a[i])
                    if n:
                        ins.add(n)
            need = max([hot.get(n, 0) for n in ins], default=0)
            while need > 0:
                v.engine_nop()
                self._decay(1)
                need = max([hot.get(n, 0) for n in ins], default=0)
            r = fn(*a, **k)
            self._decay(self.CREDIT)
            out = self._tname(a[oidx]) if oidx < len(a) else None
            if out:
                hot[out] = self.W_SLOW if name in self._SLOW else self.W_FAST
            return r
        return wrap


def build_nc(phases=99):
    nc = bass.Bass("TRN2", target_bir_lowering=False)
    x_in = nc.dram_tensor("x", [T, D], F32, kind="ExternalInput").ap()
    wqkv = nc.dram_tensor("wqkv", [3 * D, D], F32, kind="ExternalInput").ap()
    wout = nc.dram_tensor("wout", [D, D], F32, kind="ExternalInput").ap()
    wg_in = nc.dram_tensor("wg", [MID, D], F32, kind="ExternalInput").ap()
    wu_in = nc.dram_tensor("wu", [MID, D], F32, kind="ExternalInput").ap()
    wd_in = nc.dram_tensor("wd", [D, MID], F32, kind="ExternalInput").ap()
    wsg = nc.dram_tensor("wsg", [MID // 8, D], F32, kind="ExternalInput").ap()
    wsu = nc.dram_tensor("wsu", [MID // 8, D], F32, kind="ExternalInput").ap()
    wsd = nc.dram_tensor("wsd", [D // 8, MID], F32, kind="ExternalInput").ap()
    sel_in = nc.dram_tensor("sel", [8, 4], F32, kind="ExternalInput").ap()
    out = nc.dram_tensor("out", [T, D], F32, kind="ExternalOutput").ap()
    ag_in = nc.dram_tensor("ag_in", [1, AGW], F32)
    ag_out = nc.dram_tensor("ag_out", [8, AGW], F32)
    a_b = nc.dram_tensor("a_b", [MC, 128, T], F32)
    xa_b = nc.dram_tensor("xa_b", [T, D], F32)
    xq_b = nc.dram_tensor("xq_b", [TT, 128, D], BF16)

    dbg = {}
    if DEBUG:
        for name, shape, dt in [
            ("dbg_xnfm", [128, TT * D], BF16), ("dbg_xqfm", [TT * 128, D], BF16),
            ("dbg_k", [128, DC * T], BF16), ("dbg_kv", [128, DC * T], BF16),
            ("dbg_ao", [128, DC * T], BF16), ("dbg_ktot", [128, DC], F32),
            ("dbg_ag", [8, 2048], F32), ("dbg_pk", [128, DC], F32),
            ("dbg_sw", [128, 24], F32), ("dbg_sa", [128, T], F32),
            ("dbg_xa", [T, D], F32), ("dbg_tp", [128, T], F32),
            ("dbg_y", [128, MID], F32), ("dbg_xq2", [128, MID], BF16),
        ]:
            dbg[name] = nc.dram_tensor(name, shape, dt, kind="ExternalOutput").ap()

    C = {}

    def dma(s):
        C[s.name] = C.get(s.name, 0) + 16
        return C[s.name]

    def eng(s):
        C[s.name] = C.get(s.name, 0) + 1
        return C[s.name]

    def cur(s):
        return C.get(s.name, 0)

    es = contextlib.ExitStack()
    with es:
        sema = lambda name: es.enter_context(nc.semaphore(name))
        s_ldx = sema("s_ldx")
        s_ldw = sema("s_ldw")
        s_cast = sema("s_cast")
        s_tr = sema("s_tr")
        s_bw = sema("s_bw")
        s_br = sema("s_br")
        s_cc = sema("s_cc")
        s_ve = sema("s_ve")
        s_se = sema("s_se")
        s_te = sema("s_te")
        s_out = sema("s_out")
        s_t1 = [sema("s_t1a"), sema("s_t1b")]
        s_t5 = [sema("s_t5a"), sema("s_t5b"), sema("s_t5c")]
        s_t7 = [sema("s_t7a"), sema("s_t7b")]

        sbp = lambda name, shape, dt, **kw: es.enter_context(nc.sbuf_tensor(name, shape, dt, **kw))
        ones = sbp("ones", [128, T], F32)
        ones1 = sbp("ones1", [1, 128], F32)
        ones8 = sbp("ones8", [8, 1], F32)
        selb = sbp("selb", [8, 4], F32)
        agb = sbp("agb", [8, 2048], F32)
        agw = sbp("agw", [8, 4], F32)
        h128 = sbp("h128", [128, 128], F32)
        bd64 = sbp("bd64", [128, 128], F32)
        i128f = sbp("i128f", [128, 128], F32)
        ones128 = sbp("ones128", [128, 128], F32)
        dgbuf = sbp("dgbuf", [128, 128], F32)
        scol = sbp("scol", [128, 32], F32)
        sainv = sbp("sainv", [128, 32], F32)
        qmul = sbp("qmul", [128, 32], F32)
        fcol = sbp("fcol", [128, 32], F32)
        vtmp = sbp("vtmp", [128, 64], F32)
        stmp = sbp("stmp", [128, 16], F32)
        swcol = sbp("swcol", [128, 24], F32)
        negsw = sbp("negsw", [128, 12], F32)
        pk = sbp("pk", [128, DC], F32)
        pkv = sbp("pkv", [128, DC], F32)
        ktot = sbp("ktot", [128, DC], F32)
        kvtot = sbp("kvtot", [128, DC], F32)
        ktot2 = sbp("ktot2", [128, DC], F32)
        kvtot2 = sbp("kvtot2", [128, DC], F32)
        ktotF = sbp("ktotF", [128, DC], F32)
        kvtotF = sbp("kvtotF", [128, DC], F32)
        sab = sbp("sab", [128, T], F32)
        sarow = sbp("sarow", [1, T], F32)
        wsum3 = sbp("wsum3", [4, 4], F32)
        msml = sbp("msml", [4, 1], F32)
        msro = sbp("msro", [1, 4], F32)
        totrow = sbp("totrow", [16, 256], F32)
        irow = sbp("irow", [128, 128], I32)
        icol = sbp("icol", [128, 1], I32)
        ib1 = sbp("ib1", [128, 128], I32)
        ib2 = sbp("ib2", [128, 128], I32)
        ib3 = sbp("ib3", [128, 128], I32)

        pb = [es.enter_context(nc.psum_tensor(f"pb{i}", [128, 512], F32)) for i in range(6)]
        pcol = es.enter_context(nc.psum_tensor("pcol", [128, 32], F32))
        pcol2 = es.enter_context(nc.psum_tensor("pcol2", [128, 256], F32))
        bank_guard = {}

        def pwait(e, key):
            if key in bank_guard:
                smm, val = bank_guard[key]
                e.wait_ge(smm, val)

        _blk0 = es.enter_context(nc.Block())
        _vstate = {"hot": {}}

        def _bv(fn):
            return _blk0.vector(lambda v: fn(_VG(v, _vstate)))

        class blk:
            vector = staticmethod(_bv)
            scalar = _blk0.scalar
            sync = _blk0.sync
            tensor = _blk0.tensor
            gpsimd = _blk0.gpsimd

        @blk.gpsimd
        def _(g):
            g.iota(irow[:, :], [[1, 128]], channel_multiplier=0,
                   allow_small_or_imprecise_dtypes=True)
            g.iota(icol[:, :], [[1, 1]], channel_multiplier=1,
                   allow_small_or_imprecise_dtypes=True)
            g.engine_nop().then_inc(s_cc, 1)
        iota_ev = eng(s_cc)

        @blk.vector
        def _(v):
            v.memset(ones[:], 1.0)
            v.memset(ones1[:], 1.0)
            v.memset(ones8[:], 1.0)
            v.memset(ones128[:], 1.0)
            v.wait_ge(s_cc, iota_ev)
            icb = icol[:, 0:1].broadcast_to([128, 128])
            v.tensor_tensor(ib1[:], irow[:], icb, ALU.bitwise_and)
            for sh in (1, 2, 4):
                v.tensor_scalar(ib2[:], ib1[:], sh, None, ALU.logical_shift_right)
                v.tensor_tensor(ib3[:], ib1[:], ib2[:], ALU.bitwise_xor)
                v.tensor_scalar(ib1[:], ib3[:], 0, None, ALU.bitwise_or)
            v.tensor_scalar(ib1[:], ib1[:], 1, None, ALU.bitwise_and)
            v.tensor_scalar(h128[:], ib1[:], -2.0, 1.0, ALU.mult, ALU.add)
            v.tensor_scalar(ib1[:], irow[:], 63, None, ALU.bitwise_and)
            v.tensor_scalar(ib2[:], icb, 63, None, ALU.bitwise_and)
            v.tensor_tensor(ib1[:], ib1[:], ib2[:], ALU.bitwise_and)
            for sh in (1, 2, 4):
                v.tensor_scalar(ib2[:], ib1[:], sh, None, ALU.logical_shift_right)
                v.tensor_tensor(ib3[:], ib1[:], ib2[:], ALU.bitwise_xor)
                v.tensor_scalar(ib1[:], ib3[:], 0, None, ALU.bitwise_or)
            v.tensor_scalar(ib1[:], ib1[:], 1, None, ALU.bitwise_and)
            v.tensor_scalar(bd64[:], ib1[:], -2.0, 1.0, ALU.mult, ALU.add)
            v.tensor_scalar(ib1[:], irow[:], 6, None, ALU.logical_shift_right)
            v.tensor_scalar(ib2[:], icb, 6, None, ALU.logical_shift_right)
            v.tensor_scalar(ib3[:], ib1[:], 0, None, ALU.bitwise_or)
            v.tensor_tensor(ib1[:], ib3[:], ib2[:], ALU.is_equal)
            v.tensor_tensor(dgbuf[:], bd64[:], ib1[:], ALU.mult)
            v.tensor_copy(bd64[:], dgbuf[:])
            v.tensor_tensor(ib1[:], irow[:], icb, ALU.is_equal)
            v.tensor_copy(i128f[:], ib1[:]).then_inc(s_ve, 1)
        const_ev = eng(s_ve)

        @blk.sync
        def _(s):
            s.dma_start(selb[:], sel_in).then_inc(s_ldx, 16)
        sel_ld = dma(s_ldx)

        with contextlib.ExitStack() as ws:
            wtile = ws.enter_context(nc.sbuf_tensor("wtile", [128, 2048], F32))
            wtile2 = ws.enter_context(nc.sbuf_tensor("wtile2", [128, 2048], F32))
            wred = ws.enter_context(nc.sbuf_tensor("wred", [128, 4], F32))
            wps = ws.enter_context(nc.sbuf_tensor("wps", [128, 8], F32))
            slice_ev = []
            plans = ([(wsg[i * 128:(i + 1) * 128, :], 0, i == 0) for i in range(8)]
                     + [(wsu[i * 128:(i + 1) * 128, :], 1, i == 0) for i in range(8)]
                     + [(wsd[(i // 4) * 128:(i // 4 + 1) * 128,
                             (i % 4) * 2048:(i % 4 + 1) * 2048], 2, i == 0)
                        for i in range(8)])
            for i, (ap, colx, first) in enumerate(plans):
                buf = wtile if i % 2 == 0 else wtile2

                @blk.sync
                def _(s, ap=ap, buf=buf, i=i):
                    if i >= 2:
                        s.wait_ge(s_ve, slice_ev[i - 2])
                    s.dma_start(buf[:], ap).then_inc(s_ldw, 16)
                ldv = dma(s_ldw)

                sub = i % 8

                @blk.vector
                def _(v, buf=buf, colx=colx, sub=sub, ldv=ldv):
                    v.wait_ge(s_ldw, ldv)
                    v.tensor_reduce(wred[:, 0:1], buf[:], AX.X, ALU.add,
                                    apply_absolute_value=True)
                    if sub == 0:
                        v.tensor_copy(wps[:, colx:colx + 1], wred[:, 0:1])
                    elif sub % 2 == 1:
                        v.tensor_tensor(wps[:, 4 + colx:5 + colx], wps[:, colx:colx + 1],
                                        wred[:, 0:1], ALU.add)
                    else:
                        v.tensor_tensor(wps[:, colx:colx + 1], wps[:, 4 + colx:5 + colx],
                                        wred[:, 0:1], ALU.add)
                    v.engine_nop().then_inc(s_ve, 1)
                slice_ev.append(eng(s_ve))
            wps_done = slice_ev[-1]

            @blk.vector
            def _(v):
                v.memset(wps[:, 7:8], 0.0)
                v.engine_nop().then_inc(s_ve, 1)
            wps_z = eng(s_ve)

            @blk.tensor
            def _(t):
                t.wait_ge(s_ve, max(wps_done, wps_z))
                _mm = t.matmul(pcol[0:4, 0:4], wps[:, 4:8], ones[:, 0:4], start=True, stop=True)
                _mm.then_inc(s_te, 1)
            wsum_mm = eng(s_te)

            @blk.vector
            def _(v):
                v.wait_ge(s_te, wsum_mm)
                v.tensor_copy(wsum3[:, :], pcol[0:4, 0:4]).then_inc(s_ve, 1)
            wsum_ev = eng(s_ve)
        if phases <= 1:
            return nc
            bank_guard["pcol"] = (s_ve, wsum_ev)

        try:
            child_stacks = []

            class _Done(Exception):
                pass

            sQ = contextlib.ExitStack()
            q_fm = sQ.enter_context(nc.sbuf_tensor("q_fm", [128, DC * T], BF16))
            child_stacks.append(sQ)
            sK = contextlib.ExitStack()
            k_fm = sK.enter_context(nc.sbuf_tensor("k_fm", [128, DC * T], BF16))
            kv_fm = sK.enter_context(nc.sbuf_tensor("kv_fm", [128, DC * T], BF16))
            child_stacks.append(sK)
            s01 = contextlib.ExitStack()
            xn_fm = s01.enter_context(nc.sbuf_tensor("xn_fm", [128, TT * D], BF16, is_transpose_dest=True))
            child_stacks.append(s01)

            with contextlib.ExitStack() as s0s:
                xw = s0s.enter_context(nc.sbuf_tensor("xw", [128, D], F32))
                xnf = s0s.enter_context(nc.sbuf_tensor("xnf", [128, D], F32))
                xnbf = s0s.enter_context(nc.sbuf_tensor("xnbf", [128, D], BF16))
                xqbf = s0s.enter_context(nc.sbuf_tensor("xqbf", [128, D], BF16))
                xq8 = s0s.enter_context(nc.sbuf_tensor("xq8", [128, D], I8))
                xqt = s0s.enter_context(nc.sbuf_tensor("xqt", [128, D], BF16, is_transpose_dest=True))
                x_ev = {}
                tr_ev = {}
                xqw_ev = {}
                for tt in range(TT):
                    xbuf = xw
                    nb, qb = xnbf, xqbf
                    xqd = xqt
                    tmp = (tt % 2) * 32
                    stp = (tt % 2) * 8

                    @blk.sync
                    def _(s, tt=tt, xbuf=xbuf):
                        if tt >= 1:
                            s.wait_ge(s_ve, x_ev[tt - 1])
                        s.dma_start(xbuf[:], x_in[tt * 128:(tt + 1) * 128, :]).then_inc(s_ldx, 16)
                    ldv = dma(s_ldx)

                    @blk.vector
                    def _(v, xbuf=xbuf, tmp=tmp, ldv=ldv):
                        v.wait_ge(s_ldx, ldv)
                        v.tensor_reduce(vtmp[:, tmp:tmp + 1], xbuf[:], AX.X, ALU.add)
                        v.tensor_scalar(vtmp[:, tmp:tmp + 1], vtmp[:, tmp:tmp + 1],
                                        1.0 / D, None, ALU.mult).then_inc(s_ve, 1)
                    mean_ev = eng(s_ve)

                    @blk.scalar
                    def _(s, xbuf=xbuf, stp=stp, ldv=ldv, tt=tt):
                        s.wait_ge(s_ldx, ldv)
                        if tt >= 1:
                            s.wait_ge(s_ve, x_ev[tt - 1])
                        s.activation(xnf[:], xbuf[:], AF.Square,
                                     accum_out=stmp[:, stp:stp + 1]).then_inc(s_se, 1)
                    sq_ev = eng(s_se)

                    @blk.vector
                    def _(v, tmp=tmp, stp=stp, sq_ev=sq_ev):
                        v.wait_ge(s_se, sq_ev)
                        v.tensor_scalar(vtmp[:, tmp + 4:tmp + 4 + 1], stmp[:, stp:stp + 1],
                                        1.0 / D, None, ALU.mult)
                        v.tensor_copy(vtmp[:, tmp + 28:tmp + 28 + 1], vtmp[:, tmp:tmp + 1])
                        v.tensor_tensor(vtmp[:, tmp + 8:tmp + 8 + 1], vtmp[:, tmp:tmp + 1],
                                        vtmp[:, tmp + 28:tmp + 28 + 1], ALU.mult)
                        v.tensor_tensor(vtmp[:, tmp + 12:tmp + 12 + 1], vtmp[:, tmp + 4:tmp + 4 + 1],
                                        vtmp[:, tmp + 8:tmp + 8 + 1], ALU.subtract)
                        v.tensor_scalar(vtmp[:, tmp + 12:tmp + 12 + 1], vtmp[:, tmp + 12:tmp + 12 + 1],
                                        EPS_LN, None, ALU.add).then_inc(s_ve, 1)
                    var_ev = eng(s_ve)

                    @blk.scalar
                    def _(s, tmp=tmp, stp=stp, var_ev=var_ev):
                        s.wait_ge(s_ve, var_ev)
                        s.activation(stmp[:, stp + 4:stp + 4 + 1], vtmp[:, tmp + 12:tmp + 12 + 1],
                                     AF.Sqrt).then_inc(s_se, 1)
                    sqrt_ev = eng(s_se)

                    @blk.vector
                    def _(v, xbuf=xbuf, nb=nb, qb=qb, tt=tt, tmp=tmp, stp=stp, sqrt_ev=sqrt_ev):
                        v.wait_ge(s_se, sqrt_ev)
                        if tt >= 1 and KCUT >= 2:
                            v.wait_ge(s_tr, tr_ev[tt - 1])
                        v.reciprocal(vtmp[:, tmp + 16:tmp + 16 + 1], stmp[:, stp + 4:stp + 4 + 1])
                        v.tensor_scalar(xnf[:], xbuf[:], vtmp[:, tmp:tmp + 1],
                                        vtmp[:, tmp + 16:tmp + 16 + 1], ALU.subtract, ALU.mult)
                        v.tensor_copy(nb[:], xnf[:])
                        if KCUT >= 1:
                            v.tensor_reduce(vtmp[:, tmp + 20:tmp + 20 + 1], xnf[:], AX.X, ALU.max,
                                            apply_absolute_value=True)
                            v.tensor_scalar(vtmp[:, tmp + 20:tmp + 20 + 1], vtmp[:, tmp + 20:tmp + 20 + 1],
                                            EPS_AQ, None, ALU.max)
                            v.tensor_scalar(sainv[:, 4 * tt:4 * tt + 1], vtmp[:, tmp + 20:tmp + 20 + 1],
                                            1.0 / 128.0, None, ALU.mult)
                            v.reciprocal(vtmp[:, tmp + 24:tmp + 24 + 1], vtmp[:, tmp + 20:tmp + 20 + 1])
                            v.tensor_scalar(scol[:, 4 * tt:4 * tt + 1], vtmp[:, tmp + 24:tmp + 24 + 1],
                                            128.0, None, ALU.mult)
                            v.tensor_scalar(xq8[:], xnf[:], scol[:, 4 * tt:4 * tt + 1], None, ALU.mult)
                            v.tensor_copy(qb[:], xq8[:])
                        v.engine_nop().then_inc(s_ve, 1)
                    x_ev[tt] = eng(s_ve)

                    if KCUT >= 2:
                        @blk.sync
                        def _(sy, tt=tt, nb=nb, qb=qb, xqd=xqd):
                            sy.wait_ge(s_ve, x_ev[tt])
                            if tt >= 1 and KCUT >= 3:
                                sy.wait_ge(s_bw, xqw_ev[tt - 1])
                            sy.dma_start(
                                xn_fm[:, tt * D:(tt + 1) * D].rearrange("p (c j) -> p c j", j=128),
                                nb[:].rearrange("p (c j) -> p c j", j=128),
                                transpose=True).then_inc(s_tr, 16)
                            sy.dma_start(
                                xqd[:].rearrange("p (c j) -> p c j", j=128),
                                qb[:].rearrange("p (c j) -> p c j", j=128),
                                transpose=True).then_inc(s_tr, 16)
                        dma(s_tr)
                        tr_ev[tt] = dma(s_tr)

                    if KCUT >= 3:
                        @blk.sync
                        def _(s, tt=tt, xqd=xqd):
                            s.wait_ge(s_tr, tr_ev[tt])
                            s.dma_start(xq_b.ap()[tt], xqd[:]).then_inc(s_bw, 16)
                        xqw_ev[tt] = dma(s_bw)
                s0_tr = cur(s_tr)
                s0_ve = cur(s_ve)
                s0_bw = cur(s_bw)

            if KCUT < 4:
                raise _Done()
            satr_prev = None
            dg_mm = {}
            for tt in range(TT):
                @blk.vector
                def _(v, tt=tt, satr_prev=satr_prev):
                    if tt == 0:
                        v.wait_ge(s_ve, const_ev)
                    if satr_prev is not None and tt >= 2:
                        v.wait_ge(s_te, dg_mm[tt - 2])
                    v.tensor_tensor(dgbuf[:], i128f[:],
                                    sainv[:, 4 * tt:4 * tt + 1].broadcast_to([128, 128]), ALU.mult)
                    v.engine_nop().then_inc(s_ve, 1)
                dg_ev = eng(s_ve)

                @blk.tensor
                def _(t, tt=tt, dg_ev=dg_ev):
                    t.wait_ge(s_ve, dg_ev)
                    slot = tt % 2
                    pwait(t, slot)
                    _mm = t.matmul(pb[slot][:, 0:128], ones128[:], dgbuf[:], start=True, stop=True)
                    _mm.then_inc(s_te, 1)
                dg_mm[tt] = eng(s_te)

                @blk.vector
                def _(v, tt=tt):
                    v.wait_ge(s_te, dg_mm[tt])
                    v.tensor_copy(sab[:, tt * 128:(tt + 1) * 128], pb[tt % 2][:, 0:128])
                    v.engine_nop().then_inc(s_ve, 1)
                satr_prev = eng(s_ve)
                bank_guard[tt % 2] = (s_ve, satr_prev)
            sab_ev = satr_prev
            if KCUT < 5:
                raise _Done()


            if DEBUG:
                @blk.sync
                def _(s):
                    s.wait_ge(s_tr, s0_tr)
                    s.dma_start(dbg["dbg_xnfm"], xn_fm[:]).then_inc(s_out, 16)
                    s.wait_ge(s_bw, s0_bw)
                    s.dma_start(dbg["dbg_xqfm"],
                                xq_b.ap().rearrange("t p d -> (t p) d")).then_inc(s_out, 16)
                    s.wait_ge(s_ve, sab_ev)
                    s.dma_start(dbg["dbg_sa"], sab[:]).then_inc(s_out, 16)
                for _i in range(3):
                    dma(s_out)

            if phases <= 2:
                raise _Done()
            with contextlib.ExitStack() as s1s:
                wnat = [s1s.enter_context(nc.sbuf_tensor(f"wnat{i}", [128, D], BF16)) for i in range(3)]
                wslab = [s1s.enter_context(nc.sbuf_tensor(f"wslab{i}", [128, D], BF16, is_transpose_dest=True)) for i in range(3)]
                wslab2 = [s1s.enter_context(nc.sbuf_tensor(f"wslab2{i}", [128, D], BF16, is_transpose_dest=True)) for i in range(3)]
                kphi = s1s.enter_context(nc.sbuf_tensor("kphi", [128, 512], F32))
                kvf = s1s.enter_context(nc.sbuf_tensor("kvf", [128, 512], F32))
                phm = s1s.enter_context(nc.sbuf_tensor("phm", [128, 512], F32))
                phe = s1s.enter_context(nc.sbuf_tensor("phe", [128, 512], F32))
                red1 = s1s.enter_context(nc.sbuf_tensor("red1", [128, 1], F32))

                xn_view = xn_fm[:].rearrange("p (tt c j) -> p tt c j", c=DC, j=128)
                mm_done = {}
                trw_ev = {}
                cast_ev = {}
                prev_qexp = None
                for ob in range(DC):
                    slabs = wslab if ob % 2 == 0 else wslab2
                    for t3, rbase in enumerate((ob * 128, D + ob * 128, 2 * D + ob * 128)):
                        @blk.gpsimd
                        def _(g, t3=t3, rbase=rbase, ob=ob):
                            if ob == 0 and t3 == 0:
                                g.wait_ge(s_ve, s0_ve)
                                g.wait_ge(s_tr, s0_tr)
                                g.wait_ge(s_bw, s0_bw)
                            if ob >= 1:
                                g.wait_ge(*trw_ev[(ob - 1, t3)])
                            g.dma_start(wnat[t3][:], wqkv[rbase:rbase + 128, :]).then_inc(s_cast, 16)
                        cast_ev[(ob, t3)] = dma(s_cast)

                        @blk.sync
                        def _(sy, t3=t3, slabs=slabs, ob=ob, cv=cast_ev[(ob, t3)]):
                            sy.wait_ge(s_cast, cv)
                            if ob >= 2:
                                sy.wait_ge(s_te, mm_done[ob - 2])
                            sy.dma_start(slabs[t3][:].rearrange("p (c j) -> p c j", j=128),
                                         wnat[t3][:].rearrange("p (c j) -> p c j", j=128),
                                         transpose=True).then_inc(s_t1[ob % 2], 16)
                        trw_ev[(ob, t3)] = (s_t1[ob % 2], dma(s_t1[ob % 2]))

                    for tb in range(2):
                        base = (ob * 2 + tb) * 3
                        si = [base % 6, (base + 1) % 6, (base + 2) % 6]

                        @blk.tensor
                        def _(t, ob=ob, tb=tb, slabs=slabs, si=si, tr3=trw_ev[(ob, 2)]):
                            t.wait_ge(*tr3)
                            for t3 in range(3):
                                pwait(t, si[t3])
                                for c in range(DC):
                                    _mm = t.matmul(pb[si[t3]][:, 0:512],
                                             slabs[t3][:, c * 128:(c + 1) * 128],
                                             xn_view[:, tb * 4:(tb + 1) * 4, c, :],
                                             start=(c == 0), stop=(c == DC - 1))
                                _mm.then_inc(s_te, 1)
                        q_mm = eng(s_te)
                        k_mm = eng(s_te)
                        v_mm = eng(s_te)
                        mm_done[ob] = v_mm

                        @blk.vector
                        def _(v, si=si, k_mm=k_mm, prev_qexp=prev_qexp):
                            v.wait_ge(s_te, k_mm)
                            if prev_qexp is not None:
                                v.wait_ge(s_se, prev_qexp)
                            v.tensor_scalar(phm[:], pb[si[1]][:, 0:512], 0.0, None, ALU.min).then_inc(s_ve, 1)
                        mn_ev = eng(s_ve)

                        @blk.scalar
                        def _(s, mn_ev=mn_ev):
                            s.wait_ge(s_ve, mn_ev)
                            s.activation(phe[:], phm[:], AF.Exp).then_inc(s_se, 1)
                        exp_ev = eng(s_se)

                        @blk.vector
                        def _(v, ob=ob, tb=tb, si=si, exp_ev=exp_ev, v_mm=v_mm,
                              first=(tb == 0)):
                            v.wait_ge(s_se, exp_ev)
                            v.wait_ge(s_te, v_mm)
                            off = ob * T + tb * 512
                            v.scalar_tensor_tensor(kphi[:], pb[si[1]][:, 0:512], 0.0, phe[:],
                                                   ALU.max, ALU.add)
                            v.tensor_copy(k_fm[:, off:off + 512], kphi[:])
                            kdst = ktot if first else ktot2
                            kvdst = kvtot if first else kvtot2
                            v.tensor_reduce(kdst[:, ob:ob + 1], kphi[:], AX.X, ALU.add)
                            v.tensor_tensor(kvf[:], kphi[:], pb[si[2]][:, 0:512], ALU.mult)
                            v.tensor_copy(kv_fm[:, off:off + 512], kvf[:])
                            v.tensor_reduce(kvdst[:, ob:ob + 1], kvf[:], AX.X, ALU.add)
                            v.engine_nop().then_inc(s_ve, 1)
                        kv_ev = eng(s_ve)
                        bank_guard[si[1]] = (s_ve, kv_ev)
                        bank_guard[si[2]] = (s_ve, kv_ev)

                        @blk.vector
                        def _(v, si=si, q_mm=q_mm):
                            v.wait_ge(s_te, q_mm)
                            v.tensor_scalar(phm[:], pb[si[0]][:, 0:512], 0.0, None, ALU.min).then_inc(s_ve, 1)
                        qmn_ev = eng(s_ve)

                        @blk.scalar
                        def _(s, qmn_ev=qmn_ev):
                            s.wait_ge(s_ve, qmn_ev)
                            s.activation(phe[:], phm[:], AF.Exp).then_inc(s_se, 1)
                        qexp_ev = eng(s_se)
                        prev_qexp = qexp_ev

                        @blk.vector
                        def _(v, ob=ob, tb=tb, si=si, qexp_ev=qexp_ev):
                            v.wait_ge(s_se, qexp_ev)
                            off = ob * T + tb * 512
                            v.scalar_tensor_tensor(q_fm[:, off:off + 512], pb[si[0]][:, 0:512],
                                                   0.0, phe[:], ALU.max, ALU.add)
                            v.engine_nop().then_inc(s_ve, 1)
                        q_ev = eng(s_ve)
                        bank_guard[si[0]] = (s_ve, q_ev)
                s1_ve = cur(s_ve)

                if DEBUG:
                    @blk.sync
                    def _(s):
                        s.wait_ge(s_ve, s1_ve)
                        s.dma_start(dbg["dbg_k"], k_fm[:]).then_inc(s_out, 16)
                        s.dma_start(dbg["dbg_kv"], kv_fm[:]).then_inc(s_out, 16)
                    for _i in range(2):
                        dma(s_out)

                @blk.vector
                def _(v):
                    v.tensor_tensor(ktotF[:], ktot[:], ktot2[:], ALU.add)
                    v.tensor_tensor(kvtotF[:], kvtot[:], kvtot2[:], ALU.add)
                    v.engine_nop().then_inc(s_ve, 1)
                totm_ev = eng(s_ve)

                if DEBUG:
                    @blk.sync
                    def _(s):
                        s.wait_ge(s_ve, totm_ev)
                        s.dma_start(dbg["dbg_ktot"], ktotF[:]).then_inc(s_out, 16)
                    dma(s_out)

                @blk.tensor
                def _(t):
                    t.wait_ge(s_ve, totm_ev)
                    pwait(t, "pcol2")
                    _mm = t.matmul(pcol2[0:16, 0:128], ktotF[:, 0:16], i128f[:], start=True, stop=True)
                    _mm.then_inc(s_te, 1)
                ktr = eng(s_te)

                @blk.vector
                def _(v):
                    v.wait_ge(s_te, ktr)
                    v.tensor_copy(totrow[0:16, 0:128], pcol2[0:16, 0:128]).then_inc(s_ve, 1)
                ktr_cp = eng(s_ve)

                @blk.tensor
                def _(t):
                    t.wait_ge(s_ve, ktr_cp)
                    _mm = t.matmul(pcol2[0:16, 0:128], kvtotF[:, 0:16], i128f[:], start=True, stop=True)
                    _mm.then_inc(s_te, 1)
                kvtr = eng(s_te)

                @blk.vector
                def _(v):
                    v.wait_ge(s_te, kvtr)
                    v.tensor_copy(totrow[0:16, 128:256], pcol2[0:16, 0:128]).then_inc(s_ve, 1)
                kvtr_cp = eng(s_ve)
                bank_guard["pcol2"] = (s_ve, kvtr_cp)

                @blk.gpsimd
                def _(g):
                    g.wait_ge(s_ve, kvtr_cp)
                    g.dma_start(ag_in.ap()[0:1, 0:2048].rearrange("o (c p) -> (o c) p", p=128),
                                totrow[0:16, 0:128]).then_inc(s_cast, 16)
                    g.dma_start(ag_in.ap()[0:1, 2048:4096].rearrange("o (c p) -> (o c) p", p=128),
                                totrow[0:16, 128:256]).then_inc(s_cast, 16)
                    g.wait_ge(s_ve, wsum_ev)
                    g.dma_start(ag_in.ap()[0:1, 4096:4100].rearrange("o (f u) -> (o f) u", u=1),
                                wsum3[0:4, 0:1]).then_inc(s_cast, 16)
                for _i in range(3):
                    dma(s_cast)
                ag_w = cur(s_cast)

                @blk.gpsimd
                def _(g):
                    g.wait_ge(s_cast, ag_w)
                    g.collective_compute(
                        "AllGather", ALU.bypass,
                        replica_groups=[list(range(NCORE))],
                        ins=[ag_in.ap().opt()],
                        outs=[ag_out.ap().opt()],
                    ).then_inc(s_cc, 1)
                cc_ev = eng(s_cc)

                @blk.gpsimd
                def _(g):
                    g.wait_ge(s_cc, cc_ev)
                    g.dma_start(agb[:], ag_out.ap()[:, 0:2048]).then_inc(s_cast, 16)
                    g.dma_start(agw[:], ag_out.ap()[:, 4096:4100]).then_inc(s_cast, 16)
                ag_ld = dma(s_cast)
                agw_ld = dma(s_cast)

                @blk.tensor
                def _(t):
                    t.wait_ge(s_cast, agw_ld)
                    t.wait_ge(s_ldx, sel_ld)
                    pwait(t, "pcol")
                    for c in range(DC):
                        _mm = t.matmul(pcol[:, c:c + 1], agb[0:8, c * 128:(c + 1) * 128],
                                 selb[0:8, 0:1], start=True, stop=True)
                    _mm.then_inc(s_te, 1)
                pkmm = eng(s_te)

                @blk.vector
                def _(v):
                    v.wait_ge(s_te, pkmm)
                    v.tensor_copy(pk[:, :], pcol[:, 0:DC]).then_inc(s_ve, 1)
                pk_ev = eng(s_ve)

                @blk.gpsimd
                def _(g):
                    g.wait_ge(s_ve, pk_ev)
                    g.dma_start(agb[:], ag_out.ap()[:, 2048:4096]).then_inc(s_cast, 16)
                ag_ld2 = dma(s_cast)

                @blk.tensor
                def _(t):
                    t.wait_ge(s_cast, ag_ld2)
                    for c in range(DC):
                        _mm = t.matmul(pcol[:, c:c + 1], agb[0:8, c * 128:(c + 1) * 128],
                                 selb[0:8, 0:1], start=True, stop=True)
                    _mm = t.matmul(pcol[0:4, 16:17], agw[0:8, 0:4], ones8[:, :],
                             start=True, stop=True)
                    _mm.then_inc(s_te, 1)
                pkvmm = eng(s_te)

                @blk.vector
                def _(v):
                    v.wait_ge(s_te, pkvmm)
                    v.tensor_copy(pkv[:, :], pcol[:, 0:DC])
                    v.tensor_copy(msml[:, :], pcol[0:4, 16:17]).then_inc(s_ve, 1)
                ms_ev = eng(s_ve)

                @blk.tensor
                def _(t):
                    t.wait_ge(s_ve, ms_ev)
                    pwait(t, "pcol2")
                    _mm = t.matmul(pcol2[0:1, 0:4], msml[0:4, 0:1], i128f[0:4, 0:4], start=True, stop=True)
                    _mm.then_inc(s_te, 1)
                mstr = eng(s_te)

                @blk.vector
                def _(v):
                    v.wait_ge(s_te, mstr)
                    v.tensor_copy(msro[0:1, 0:4], pcol2[0:1, 0:4]).then_inc(s_ve, 1)
                msro_ev = eng(s_ve)
                bank_guard["pcol2"] = (s_ve, msro_ev)

                @blk.tensor
                def _(t):
                    t.wait_ge(s_ve, msro_ev)
                    for j in range(3):
                        _mm = t.matmul(pcol[:, 17 + j:18 + j], ones1[:, :], msro[0:1, j:j + 1],
                                 start=True, stop=True)
                    _mm.then_inc(s_te, 1)
                mbmm = eng(s_te)

                @blk.vector
                def _(v):
                    v.wait_ge(s_te, mbmm)
                    for j, n in ((0, MID * D), (1, MID * D), (2, D * MID)):
                        v.tensor_scalar(swcol[:, 12 + 4 * j:13 + 4 * j], pcol[:, 17 + j:18 + j],
                                        1.0 / n, EPS_AQ, ALU.mult, ALU.max)
                        v.reciprocal(swcol[:, 4 * j:4 * j + 1], swcol[:, 12 + 4 * j:13 + 4 * j])
                        v.tensor_scalar(negsw[:, 4 * j:4 * j + 1], swcol[:, 4 * j:4 * j + 1],
                                        -1.0, None, ALU.mult)
                    v.engine_nop().then_inc(s_ve, 1)
                sw_ev = eng(s_ve)
                bank_guard["pcol"] = (s_ve, sw_ev)

                if DEBUG:
                    @blk.sync
                    def _(s):
                        s.wait_ge(s_ve, sw_ev)
                        s.dma_start(dbg["dbg_ag"], agb[:]).then_inc(s_out, 16)
                        s.dma_start(dbg["dbg_pk"], pk[:]).then_inc(s_out, 16)
                        s.dma_start(dbg["dbg_sw"], swcol[:]).then_inc(s_out, 16)
                    for _i in range(3):
                        dma(s_out)
            if phases <= 4:
                raise _Done()
            s01.close()
            child_stacks.remove(s01)

            with contextlib.ExitStack() as s3s:
                kcum = s3s.enter_context(nc.sbuf_tensor("kcum", [128, T], F32))
                kvcum = s3s.enter_context(nc.sbuf_tensor("kvcum", [128, T], F32))
                rnum = s3s.enter_context(nc.sbuf_tensor("rnum", [128, T], F32))
                rden = s3s.enter_context(nc.sbuf_tensor("rden", [128, T], F32))
                for c in range(DC):
                    @blk.vector
                    def _(v, c=c):
                        if c == 0:
                            v.wait_ge(s_ve, sw_ev)
                        v.tensor_tensor_scan(kcum[:], ones[:], k_fm[:, c * T:(c + 1) * T],
                                             pk[:, c:c + 1], ALU.mult, ALU.add)
                        v.tensor_tensor_scan(kvcum[:], ones[:], kv_fm[:, c * T:(c + 1) * T],
                                             pkv[:, c:c + 1], ALU.mult, ALU.add)
                        q = q_fm[:, c * T:(c + 1) * T]
                        v.tensor_tensor(rnum[:], kvcum[:], q, ALU.mult)
                        v.tensor_tensor(rden[:], kcum[:], q, ALU.mult)
                        v.tensor_scalar(rden[:], rden[:], EPS_ATT, None, ALU.add)
                        v.reciprocal(kcum[:], rden[:])
                        v.tensor_tensor(q, rnum[:], kcum[:], ALU.mult)
                        v.engine_nop().then_inc(s_ve, 1)
                    eng(s_ve)
                s3_ve = cur(s_ve)
                ao_fm = q_fm

                if DEBUG:
                    @blk.sync
                    def _(s):
                        s.wait_ge(s_ve, s3_ve)
                        s.dma_start(dbg["dbg_ao"], ao_fm[:]).then_inc(s_out, 16)
                    dma(s_out)
            if phases <= 5:
                raise _Done()
            sK.close()
            child_stacks.remove(sK)

            with contextlib.ExitStack() as s4s:
                woT = s4s.enter_context(nc.sbuf_tensor("woT", [128, 16 * D], BF16, is_transpose_dest=True))
                wnatA = s4s.enter_context(nc.sbuf_tensor("wnatA", [128, D], BF16))
                wnatB = s4s.enter_context(nc.sbuf_tensor("wnatB", [128, D], BF16))
                xrdA = s4s.enter_context(nc.sbuf_tensor("xrdA", [128, 512], F32))
                xrdB = s4s.enter_context(nc.sbuf_tensor("xrdB", [128, 512], F32))
                xaA = s4s.enter_context(nc.sbuf_tensor("xaA", [128, 512], F32))
                xaB = s4s.enter_context(nc.sbuf_tensor("xaB", [128, 512], F32))

                wotr_ev = {}
                for rt in range(16):
                    nb = wnatA if rt % 2 == 0 else wnatB

                    @blk.gpsimd
                    def _(g, rt=rt, nb=nb):
                        if rt == 0:
                            g.wait_ge(s_ve, s3_ve)
                        if rt >= 2:
                            g.wait_ge(s_tr, wotr_ev[rt - 2])
                        g.dma_start(nb[:], wout[rt * 128:(rt + 1) * 128, :]).then_inc(s_cast, 16)
                    cv = dma(s_cast)

                    @blk.sync
                    def _(sy, rt=rt, nb=nb, cv=cv):
                        sy.wait_ge(s_cast, cv)
                        sy.dma_start(woT[:, rt * D:(rt + 1) * D].rearrange("p (c j) -> p c j", j=128),
                                     nb[:].rearrange("p (c j) -> p c j", j=128),
                                     transpose=True).then_inc(s_tr, 16)
                    wotr_ev[rt] = dma(s_tr)
                woT_done = cur(s_tr)

                wo_view = woT[:].rearrange("p (rt c j) -> p rt c j", c=DC, j=128)
                xa_ev = {}
                xa_bw = {}
                for tt in range(TT):
                    for obk in range(4):
                        it = tt * 4 + obk
                        slot = it % 6
                        xrd = xrdA if it % 2 == 0 else xrdB
                        xab = xaA if it % 2 == 0 else xaB

                        @blk.sync
                        def _(s, tt=tt, obk=obk, xrd=xrd, it=it):
                            if it >= 2:
                                s.wait_ge(s_ve, xa_ev[it - 2])
                            s.dma_start(xrd[:], x_in[tt * 128:(tt + 1) * 128,
                                                     obk * 512:(obk + 1) * 512]).then_inc(s_ldx, 16)
                        xld = dma(s_ldx)

                        @blk.tensor
                        def _(t, tt=tt, obk=obk, slot=slot, it=it):
                            if it == 0:
                                t.wait_ge(s_tr, woT_done)
                                t.wait_ge(s_ve, s3_ve)
                            pwait(t, slot)
                            for c in range(DC):
                                _mm = t.matmul(pb[slot][:, 0:512],
                                         ao_fm[:, c * T + tt * 128: c * T + (tt + 1) * 128],
                                         wo_view[:, obk * 4:(obk + 1) * 4, c, :],
                                         start=(c == 0), stop=(c == DC - 1))
                            _mm.then_inc(s_te, 1)
                        mm_ev = eng(s_te)

                        @blk.vector
                        def _(v, slot=slot, xrd=xrd, xab=xab, mm_ev=mm_ev, xld=xld, it=it):
                            v.wait_ge(s_te, mm_ev)
                            v.wait_ge(s_ldx, xld)
                            if it >= 2:
                                v.wait_ge(s_bw, xa_bw[it - 2])
                            v.scalar_tensor_tensor(xab[:], pb[slot][:, 0:512], C_ATT, xrd[:],
                                                   ALU.mult, ALU.add)
                            v.engine_nop().then_inc(s_ve, 1)
                        ev = eng(s_ve)
                        xa_ev[it] = ev
                        bank_guard[slot] = (s_ve, ev)

                        @blk.sync
                        def _(s, tt=tt, obk=obk, xab=xab, ev=ev):
                            s.wait_ge(s_ve, ev)
                            s.dma_start(xa_b.ap()[tt * 128:(tt + 1) * 128,
                                                  obk * 512:(obk + 1) * 512],
                                        xab[:]).then_inc(s_bw, 16)
                        xa_bw[it] = dma(s_bw)
                s4_bw = cur(s_bw)
                s4_ve = cur(s_ve)

                if DEBUG:
                    @blk.sync
                    def _(s):
                        s.wait_ge(s_bw, s4_bw)
                        s.dma_start(dbg["dbg_xa"], xa_b.ap()).then_inc(s_out, 16)
                    dma(s_out)
            if phases <= 6:
                raise _Done()
            sQ.close()
            child_stacks.remove(sQ)

            with contextlib.ExitStack() as s5s:
                xqs = s5s.enter_context(nc.sbuf_tensor("xqs", [128, TT * D], BF16))
                wgf = [s5s.enter_context(nc.sbuf_tensor(f"wgf{i}", [128, D], F32)) for i in range(3)]
                wuf = [s5s.enter_context(nc.sbuf_tensor(f"wuf{i}", [128, D], F32)) for i in range(3)]
                tca = s5s.enter_context(nc.sbuf_tensor("tca", [128, D], BF16))
                tcb = s5s.enter_context(nc.sbuf_tensor("tcb", [128, D], BF16))
                wgq = [s5s.enter_context(nc.sbuf_tensor(f"wgq{i}", [128, D], BF16)) for i in range(3)]
                wuq = [s5s.enter_context(nc.sbuf_tensor(f"wuq{i}", [128, D], BF16)) for i in range(3)]
                wgT = [s5s.enter_context(nc.sbuf_tensor(f"wgT{i}", [128, D], BF16, is_transpose_dest=True)) for i in range(3)]
                wuT = [s5s.enter_context(nc.sbuf_tensor(f"wuT{i}", [128, D], BF16, is_transpose_dest=True)) for i in range(3)]
                gsb = s5s.enter_context(nc.sbuf_tensor("gsb", [128, 512], F32))
                slb = s5s.enter_context(nc.sbuf_tensor("slb", [128, 512], F32))
                tprime = [s5s.enter_context(nc.sbuf_tensor(f"tprime{i}", [128, T], F32)) for i in range(2)]
                astg = [s5s.enter_context(nc.sbuf_tensor(f"astg{i}", [128, T], F32)) for i in range(2)]

                @blk.sync
                def _(s):
                    s.wait_ge(s_ve, s4_ve)
                    s.wait_ge(s_bw, s0_bw)
                    for tt in range(TT):
                        s.dma_start(xqs[:, tt * D:(tt + 1) * D], xq_b.ap()[tt]).then_inc(s_br, 16)
                for _i in range(TT):
                    dma(s_br)
                xqs_ld = cur(s_br)

                xq_view = xqs[:].rearrange("p (tt c j) -> p tt c j", c=DC, j=128)
                tern_ev = {}
                a_ev = {}
                aw_ev = {}
                for om in range(MC):
                    pp = om % 3
                    pq = om % 2

                    @blk.sync
                    def _(s, om=om, pp=pp):
                        if om >= 3:
                            s.wait_ge(s_ve, tern_ev[om - 3])
                        s.dma_start(wgf[pp][:], wg_in[om * 128:(om + 1) * 128, :]).then_inc(s_ldw, 16)
                        s.dma_start(wuf[pp][:], wu_in[om * 128:(om + 1) * 128, :]).then_inc(s_ldw, 16)
                    dma(s_ldw)
                    wld = dma(s_ldw)

                    @blk.vector
                    def _(v, om=om, pp=pp, wld=wld):
                        v.wait_ge(s_ldw, wld)
                        if om == 0:
                            v.wait_ge(s_ve, sw_ev)
                        if om >= 3:
                            v.wait_ge(*tern_ev[(om - 3, "tr")])
                        v.tensor_scalar(tca[:], wgf[pp][:], swcol[:, 0:1], 0.5, ALU.mult, ALU.is_gt)
                        v.tensor_scalar(tcb[:], wgf[pp][:], negsw[:, 0:1], 0.5, ALU.mult, ALU.is_gt)
                        v.tensor_tensor(wgq[pp][:], tca[:], tcb[:], ALU.subtract)
                        v.tensor_scalar(tca[:], wuf[pp][:], swcol[:, 4:5], 0.5, ALU.mult, ALU.is_gt)
                        v.tensor_scalar(tcb[:], wuf[pp][:], negsw[:, 4:5], 0.5, ALU.mult, ALU.is_gt)
                        v.tensor_tensor(wuq[pp][:], tca[:], tcb[:], ALU.subtract)
                        v.engine_nop().then_inc(s_ve, 1)
                    tern_ev[om] = eng(s_ve)

                    @blk.sync
                    def _(sy, om=om, pp=pp):
                        sy.wait_ge(s_ve, tern_ev[om])
                        if om >= 3:
                            sy.wait_ge(s_te, a_ev[om - 3])
                        sy.dma_start(wgT[pp][:].rearrange("p (c j) -> p c j", j=128),
                                     wgq[pp][:].rearrange("p (c j) -> p c j", j=128),
                                     transpose=True).then_inc(s_t5[om % 3], 16)
                        sy.dma_start(wuT[pp][:].rearrange("p (c j) -> p c j", j=128),
                                     wuq[pp][:].rearrange("p (c j) -> p c j", j=128),
                                     transpose=True).then_inc(s_t5[om % 3], 16)
                    dma(s_t5[om % 3])
                    tern_ev[(om, "tr")] = (s_t5[om % 3], dma(s_t5[om % 3]))

                    for tb in range(2):
                        sg = ((om * 2 + tb) * 2) % 6
                        su = ((om * 2 + tb) * 2 + 1) % 6

                        @blk.tensor
                        def _(t, om=om, tb=tb, pp=pp, sg=sg, su=su):
                            if tb == 0:
                                t.wait_ge(*tern_ev[(om, "tr")])
                                if om == 0:
                                    t.wait_ge(s_br, xqs_ld)
                            for t2, slot in ((0, sg), (1, su)):
                                wT = wgT[pp] if t2 == 0 else wuT[pp]
                                pwait(t, slot)
                                for c in range(DC):
                                    _mm = t.matmul(pb[slot][:, 0:512],
                                             wT[:, c * 128:(c + 1) * 128],
                                             xq_view[:, tb * 4:(tb + 1) * 4, c, :],
                                             start=(c == 0), stop=(c == DC - 1))
                                _mm.then_inc(s_te, 1)
                        g_mm = eng(s_te)
                        u_mm = eng(s_te)

                        @blk.vector
                        def _(v, sg=sg, tb=tb, g_mm=g_mm):
                            v.wait_ge(s_te, g_mm)
                            v.scalar_tensor_tensor(gsb[:], pb[sg][:, 0:512], swcol[:, 12:13],
                                                   sab[:, tb * 512:(tb + 1) * 512], ALU.mult, ALU.mult)
                            v.engine_nop().then_inc(s_ve, 1)
                        gs_ev = eng(s_ve)
                        bank_guard[sg] = (s_ve, gs_ev)

                        @blk.scalar
                        def _(s, gs_ev=gs_ev):
                            s.wait_ge(s_ve, gs_ev)
                            s.activation(slb[:], gsb[:], AF.Silu).then_inc(s_se, 1)
                        sil_ev = eng(s_se)

                        @blk.vector
                        def _(v, su=su, tb=tb, pp=pq, sil_ev=sil_ev, u_mm=u_mm, om=om):
                            v.wait_ge(s_se, sil_ev)
                            v.wait_ge(s_te, u_mm)
                            if om >= 2 and tb == 0:
                                v.wait_ge(s_bw, aw_ev[om - 2])
                            v.scalar_tensor_tensor(tprime[pp][:, tb * 512:(tb + 1) * 512],
                                                   pb[su][:, 0:512], swcol[:, 16:17], slb[:],
                                                   ALU.mult, ALU.mult)
                            v.engine_nop().then_inc(s_ve, 1)
                        tp_ev = eng(s_ve)
                        bank_guard[su] = (s_ve, tp_ev)

                        @blk.tensor
                        def _(t, om=om, tb=tb, pp=pq, tp_ev=tp_ev, slot=sg):
                            t.wait_ge(s_ve, tp_ev)
                            pwait(t, slot)
                            _mm = t.matmul(pb[slot][:, 0:512], h128[:],
                                     tprime[pp][:, tb * 512:(tb + 1) * 512], start=True, stop=True)
                            _mm.then_inc(s_te, 1)
                        amm_ev = eng(s_te)

                        @blk.vector
                        def _(v, pp=pq, tb=tb, amm_ev=amm_ev, slot=sg):
                            v.wait_ge(s_te, amm_ev)
                            v.tensor_copy(astg[pp][:, tb * 512:(tb + 1) * 512], pb[slot][:, 0:512])
                            v.engine_nop().then_inc(s_ve, 1)
                        aev = eng(s_ve)
                        bank_guard[sg] = (s_ve, aev)
                        if tb == 1:
                            a_ev[om] = amm_ev

                            @blk.sync
                            def _(s, om=om, pp=pq, aev=aev):
                                s.wait_ge(s_ve, aev)
                                s.dma_start(a_b.ap()[om], astg[pp][:]).then_inc(s_bw, 16)
                            aw_ev[om] = dma(s_bw)
                s5_bw = cur(s_bw)
                s5_ve = cur(s_ve)

                if DEBUG:
                    @blk.sync
                    def _(s):
                        s.wait_ge(s_bw, s5_bw)
                        s.dma_start(dbg["dbg_tp"], tprime[1][:]).then_inc(s_out, 16)
                    dma(s_out)

            if phases <= 7:
                raise _Done()
            sX = contextlib.ExitStack()
            xq2i = sX.enter_context(nc.sbuf_tensor("xq2i", [128, TT * MID], I8))
            child_stacks.append(sX)
            with contextlib.ExitStack() as s6s:
                ards = s6s.enter_context(nc.sbuf_tensor("ards", [128, MID], F32))
                ybuf = s6s.enter_context(nc.sbuf_tensor("ybuf", [128, MID], F32))
                yq8 = s6s.enter_context(nc.sbuf_tensor("yq8", [128, MID], I8))
                yqbf = s6s.enter_context(nc.sbuf_tensor("yqbf", [128, MID], BF16))
                yqT = s6s.enter_context(nc.sbuf_tensor("yqT", [128, MID], BF16, is_transpose_dest=True))

                a_bv = a_b.ap().rearrange("i (c e) t -> i c e t", e=2)
                rb_ev = {}
                y_ev = {}
                tq_ev = {}
                up_evs = {}
                ttr_evs = {}
                for tt in range(TT):
                    @blk.sync
                    def _(s, tt=tt):
                        if tt == 0:
                            s.wait_ge(s_bw, s5_bw)
                            s.wait_ge(s_ve, s5_ve)
                        else:
                            s.wait_ge(s_ve, y_ev[tt - 1])
                        for e in range(2):
                            for cg in range(4):
                                s.dma_start(
                                    ards[64 * e:64 * e + 64,
                                         cg * 2048:(cg + 1) * 2048].rearrange(
                                             "p (c tau) -> p c tau", tau=128),
                                    a_bv[:, cg * 16:(cg + 1) * 16, e,
                                         tt * 128:(tt + 1) * 128]).then_inc(s_br, 16)
                    for _i in range(8):
                        dma(s_br)
                    rb_ev[tt] = cur(s_br)

                    for grp in range(16):
                        slot = grp % 6

                        @blk.tensor
                        def _(t, tt=tt, grp=grp, slot=slot):
                            if grp == 0:
                                t.wait_ge(s_br, rb_ev[tt])
                                if tt == 0:
                                    t.wait_ge(s_ve, const_ev)
                            pwait(t, slot)
                            for q4 in range(4):
                                ctil = grp * 4 + q4
                                _mm = t.matmul(pb[slot][:, q4 * 128:(q4 + 1) * 128],
                                         ards[:, ctil * 128:(ctil + 1) * 128],
                                         bd64[:], start=True, stop=True)
                            _mm.then_inc(s_te, 1)
                        bmm = eng(s_te)

                        @blk.vector
                        def _(v, grp=grp, slot=slot, bmm=bmm):
                            v.wait_ge(s_te, bmm)
                            dstv = ybuf[:].rearrange("p (j r) -> p j r", r=128)
                            dstv = dstv[:, :, 8 * grp: 8 * grp + 8]
                            dstv = dstv.rearrange("p j e -> p e j")
                            v.tensor_copy(dstv, pb[slot][:, 0:512])
                            v.engine_nop().then_inc(s_ve, 1)
                        yev = eng(s_ve)
                        bank_guard[slot] = (s_ve, yev)
                    y_ev[tt] = cur(s_ve)

                    @blk.vector
                    def _(v, tt=tt, pupev=(up_evs[tt - 1] if tt >= 1 else None)):
                        tb2 = (tt % 2) * 32
                        if pupev is not None:
                            v.wait_ge(s_se, pupev)
                        v.tensor_reduce(vtmp[:, tb2 + 0:tb2 + 1], ybuf[:], AX.X, ALU.max,
                                        apply_absolute_value=True)
                        v.tensor_tensor(vtmp[:, tb2 + 4:tb2 + 5], vtmp[:, tb2 + 0:tb2 + 1],
                                        sainv[:, 4 * tt:4 * tt + 1], ALU.mult)
                        v.tensor_scalar(vtmp[:, tb2 + 4:tb2 + 5], vtmp[:, tb2 + 4:tb2 + 5],
                                        C_FWHT, EPS_AQ, ALU.mult, ALU.max)
                        v.reciprocal(vtmp[:, tb2 + 8:tb2 + 9], vtmp[:, tb2 + 4:tb2 + 5])
                        v.tensor_scalar(vtmp[:, tb2 + 8:tb2 + 9], vtmp[:, tb2 + 8:tb2 + 9],
                                        128.0, None, ALU.mult)
                        v.tensor_tensor(vtmp[:, tb2 + 12:tb2 + 13], sainv[:, 4 * tt:4 * tt + 1],
                                        vtmp[:, tb2 + 8:tb2 + 9], ALU.mult)
                        v.tensor_scalar(qmul[:, 4 * tt:4 * tt + 1], vtmp[:, tb2 + 12:tb2 + 13],
                                        C_FWHT, None, ALU.mult)
                        v.tensor_scalar(vtmp[:, tb2 + 16:tb2 + 17], vtmp[:, tb2 + 4:tb2 + 5],
                                        1.0 / 128.0, None, ALU.mult)
                        v.tensor_tensor(fcol[:, 4 * tt:4 * tt + 1], vtmp[:, tb2 + 16:tb2 + 17],
                                        swcol[:, 20:21], ALU.mult)
                        v.tensor_scalar(yq8[:], ybuf[:], qmul[:, 4 * tt:4 * tt + 1], None, ALU.mult)
                        v.engine_nop().then_inc(s_ve, 1)
                    q_ev6 = eng(s_ve)

                    @blk.scalar
                    def _(s, q_ev6=q_ev6, ptr_ev=(ttr_evs[tt - 1] if tt >= 1 else None)):
                        s.wait_ge(s_ve, q_ev6)
                        if ptr_ev is not None:
                            s.wait_ge(s_tr, ptr_ev)
                        s.copy(yqbf[:], yq8[:]).then_inc(s_se, 1)
                    up_ev = eng(s_se)
                    up_evs[tt] = up_ev

                    @blk.sync
                    def _(sy, tt=tt, up_ev=up_ev):
                        sy.wait_ge(s_se, up_ev)
                        if tt >= 1:
                            sy.wait_ge(s_ve, tq_ev[tt - 1])
                        sy.dma_start(yqT[:].rearrange("p (c j) -> p c j", j=128),
                                     yqbf[:].rearrange("p (c j) -> p c j", j=128),
                                     transpose=True).then_inc(s_tr, 16)
                    ttr = dma(s_tr)
                    ttr_evs[tt] = ttr

                    @blk.vector
                    def _(v, tt=tt, ttr=ttr):
                        v.wait_ge(s_tr, ttr)
                        v.tensor_copy(xq2i[:, tt * MID:(tt + 1) * MID], yqT[:])
                        v.engine_nop().then_inc(s_ve, 1)
                    tq_ev[tt] = eng(s_ve)

                    if DEBUG and tt == 0:
                        @blk.sync
                        def _(s, tt=tt):
                            s.wait_ge(s_ve, tq_ev[tt])
                            s.dma_start(dbg["dbg_y"], ybuf[:]).then_inc(s_out, 16)
                            s.dma_start(dbg["dbg_xq2"], yqT[:]).then_inc(s_out, 16)
                        for _i in range(2):
                            dma(s_out)
                s6_ve = cur(s_ve)

            if phases <= 8:
                raise _Done()
            with contextlib.ExitStack() as s7s:
                wdf = [s7s.enter_context(nc.sbuf_tensor(f"wdf{i}", [128, 2048], F32)) for i in range(2)]
                d_ta = s7s.enter_context(nc.sbuf_tensor("d_ta", [128, 2048], BF16))
                d_tb = s7s.enter_context(nc.sbuf_tensor("d_tb", [128, 2048], BF16))
                wqq = [s7s.enter_context(nc.sbuf_tensor(f"wqq{i}", [128, 2048], BF16)) for i in range(2)]
                wdT = [s7s.enter_context(nc.sbuf_tensor("wdT0", [128, 2 * MID], BF16, is_transpose_dest=True))] * 2
                xup = s7s.enter_context(nc.sbuf_tensor("xup", [128, MID], BF16))
                xard = [s7s.enter_context(nc.sbuf_tensor(f"xard{i}", [128, 256], F32)) for i in range(2)]
                outb = [s7s.enter_context(nc.sbuf_tensor(f"outb{i}", [128, 256], F32)) for i in range(2)]

                wdT_view = [w[:].rearrange("p (r c j) -> p r c j", c=MC, j=128) for w in wdT]
                dt_ev = {}
                dmm_ev = {}
                qtr_ev = {}
                tern_ev7 = {}
                wdT_done = {}
                dmm_hist = []
                for ob in range(8):
                    wsel = ob % 2
                    for r2 in range(2):
                        rt = ob * 2 + r2
                        for qt in range(4):
                            qq = wqq[qt % 2]
                            wb = wdf[qt % 2]
                            qi = (rt, qt)
                            prev = (rt, qt - 2) if qt >= 2 else (rt - 1, qt + 2)

                            @blk.sync
                            def _(s, wb=wb, rt=rt, qt=qt, ob=ob, r2=r2, prev=prev):
                                if ob == 0 and r2 == 0 and qt == 0:
                                    s.wait_ge(s_ve, s6_ve)
                                if prev in tern_ev7:
                                    s.wait_ge(s_ve, tern_ev7[prev])
                                s.dma_start(wb[:], wd_in[rt * 128:(rt + 1) * 128,
                                                         qt * 2048:(qt + 1) * 2048]).then_inc(s_ldw, 16)
                            wld = dma(s_ldw)

                            @blk.vector
                            def _(v, wb=wb, qq=qq, wld=wld, prev=prev):
                                v.wait_ge(s_ldw, wld)
                                if prev in qtr_ev:
                                    v.wait_ge(*qtr_ev[prev])
                                v.tensor_scalar(d_ta[:], wb[:], swcol[:, 8:9], 0.5, ALU.mult, ALU.is_gt)
                                v.tensor_scalar(d_tb[:], wb[:], negsw[:, 8:9], 0.5, ALU.mult, ALU.is_gt)
                                v.tensor_tensor(qq[:], d_ta[:], d_tb[:], ALU.subtract)
                                v.engine_nop().then_inc(s_ve, 1)
                            tern_ev7[qi] = eng(s_ve)

                            @blk.sync
                            def _(sy, r2=r2, qt=qt, qq=qq, ob=ob, qi=qi, wsel=wsel):
                                sy.wait_ge(s_ve, tern_ev7[qi])
                                if ob >= 1 and r2 == 0 and qt == 0:
                                    sy.wait_ge(s_te, dmm_ev[ob - 1])
                                sy.dma_start(
                                    wdT[wsel][:, r2 * MID + qt * 2048: r2 * MID + (qt + 1) * 2048]
                                    .rearrange("p (c j) -> p c j", j=128),
                                    qq[:].rearrange("p (c j) -> p c j", j=128),
                                    transpose=True).then_inc(s_t7[wsel], 16)
                            qtr_ev[qi] = (s_t7[wsel], dma(s_t7[wsel]))
                    wdT_done[ob] = (s_t7[wsel], cur(s_t7[wsel]))

                    for tt in range(TT):
                        it = ob * TT + tt
                        slot = it % 6
                        pp2 = it % 2

                        @blk.sync
                        def _(sy, tt=tt, ob=ob, it=it, pp2=pp2):
                            if it >= 2:
                                sy.wait_ge(s_ve, dt_ev[it - 2])
                            sy.dma_start(xard[pp2][:], xa_b.ap()[tt * 128:(tt + 1) * 128,
                                                                ob * 256:(ob + 1) * 256]).then_inc(s_br, 16)
                        _xald = 1
                        
                        @blk.scalar
                        def _(s, tt=tt, ob=ob, it=it,
                              pdm=(dmm_hist[it - 1] if it >= 1 else None)):
                            if ob == 0:
                                s.wait_ge(s_ve, tq_ev[tt])
                            if pdm is not None:
                                s.wait_ge(s_te, pdm)
                            s.copy(xup[:], xq2i[:, tt * MID:(tt + 1) * MID]).then_inc(s_se, 1)
                        xa_ld = dma(s_br)
                        up7 = eng(s_se)

                        @blk.tensor
                        def _(t, tt=tt, ob=ob, it=it, slot=slot, up7=up7, wsel=wsel):
                            t.wait_ge(s_se, up7)
                            t.wait_ge(*wdT_done[ob])
                            pwait(t, slot)
                            for c in range(MC):
                                _mm = t.matmul(pb[slot][:, 0:256],
                                         xup[:, c * 128:(c + 1) * 128],
                                         wdT_view[wsel][:, :, c, :],
                                         start=(c == 0), stop=(c == MC - 1))
                            _mm.then_inc(s_te, 1)
                        dmm = eng(s_te)
                        dmm_ev[ob] = dmm
                        dmm_hist.append(dmm)



                        @blk.vector
                        def _(v, tt=tt, slot=slot, pp2=pp2, dmm=dmm, xa_ld=xa_ld):
                            v.wait_ge(s_te, dmm)
                            v.wait_ge(s_br, xa_ld)
                            v.scalar_tensor_tensor(outb[pp2][:], pb[slot][:, 0:256],
                                                   fcol[:, 4 * tt:4 * tt + 1], xard[pp2][:],
                                                   ALU.mult, ALU.add)
                            v.engine_nop().then_inc(s_ve, 1)
                        oev = eng(s_ve)
                        dt_ev[it] = oev
                        bank_guard[slot] = (s_ve, oev)

                        @blk.sync
                        def _(s, tt=tt, ob=ob, pp2=pp2, oev=oev):
                            s.wait_ge(s_ve, oev)
                            s.dma_start(out[tt * 128:(tt + 1) * 128, ob * 256:(ob + 1) * 256],
                                        outb[pp2][:]).then_inc(s_out, 16)
                        dma(s_out)
                out_total = cur(s_out)

                @blk.scalar
                def _(s):
                    s.wait_ge(s_out, out_total)

                @blk.sync
                def _(sy):
                    sy.wait_ge(s_out, out_total)
            sX.close()
            child_stacks.remove(sX)

        except _Done:
            for st_ in reversed(child_stacks):
                st_.close()
    return nc


_NC_CACHE = {}


def kernel(L0_attn_x, L0_attn_W_qkv, L0_attn_W_out, L0_ffn_w_gate, L0_ffn_w_up,
           L0_ffn_w_down):
    x = np.ascontiguousarray(np.asarray(L0_attn_x, dtype=np.float32))
    wqkv = np.ascontiguousarray(np.asarray(L0_attn_W_qkv, dtype=np.float32))
    woutm = np.ascontiguousarray(np.asarray(L0_attn_W_out, dtype=np.float32))
    wg = np.ascontiguousarray(np.asarray(L0_ffn_w_gate, dtype=np.float32))
    wu = np.ascontiguousarray(np.asarray(L0_ffn_w_up, dtype=np.float32))
    wd = np.ascontiguousarray(np.asarray(L0_ffn_w_down, dtype=np.float32))
    B = x.shape[0]
    if "nc" not in _NC_CACHE:
        _NC_CACHE["nc"] = build_nc()
    nc = _NC_CACHE["nc"]
    in_maps = []
    for c in range(NCORE):
        b, q = c // 4, c % 4
        selm = np.zeros((8, 4), np.float32)
        for r in range(8):
            if r // 4 == b and r % 4 < q:
                selm[r, 0] = 1.0
        in_maps.append({
            "x": np.ascontiguousarray(x[b, q * T:(q + 1) * T, :]),
            "wqkv": wqkv, "wout": woutm, "wg": wg, "wu": wu, "wd": wd,
            "wsg": np.ascontiguousarray(wg[c * (MID // 8):(c + 1) * (MID // 8), :]),
            "wsu": np.ascontiguousarray(wu[c * (MID // 8):(c + 1) * (MID // 8), :]),
            "wsd": np.ascontiguousarray(wd[c * (D // 8):(c + 1) * (D // 8), :]),
            "sel": selm,
        })
    res = run_bass_kernel_spmd(nc, in_maps, core_ids=list(range(NCORE)))
    outp = np.empty((B, 4 * T, D), np.float32)
    for c in range(NCORE):
        b, q = c // 4, c % 4
        outp[b, q * T:(q + 1) * T, :] = res.results[c]["out"]
    return outp
